# version 15
# baseline (speedup 1.0000x reference)
import numpy as np
from contextlib import ExitStack

import concourse.bass as bass
import concourse.mybir as mybir
from concourse.bass_utils import run_bass_kernel_spmd

F32 = mybir.dt.float32
BF16 = mybir.dt.bfloat16
I32 = mybir.dt.int32
U8 = mybir.dt.uint8
AF = mybir.ActivationFunctionType
ALU = mybir.AluOpType
AX = mybir.AxisListType

H = 2048
D = 128
II = 8192
NHB = 16
NIT = 64
NHT = 16
T_OWN = 1024
NT_OWN = 8
NT_HIST = 24
NT = NT_HIST + NT_OWN
LNEPS = 1e-5
AEPS = 1e-6

ENGS = ("sync", "gpsimd", "scalar", "vector", "tensor")


class Mem:

    def __init__(self, nc, base):
        self.nc = nc
        self.base = base
        self.n = 0

    def at(self, off, shape, dtype):
        self.n += 1
        h = self.nc.alloc_sbuf_tensor_at(f"m{self.n}", shape, dtype,
                                         offset=self.base + off)
        return h.ap()


def build_program(debug=False):
    nc = bass.Bass("TRN2", target_bir_lowering=False)

    dri = lambda n, s, d: nc.dram_tensor(n, s, d, kind="ExternalInput").ap()
    dro = lambda n, s, d: nc.dram_tensor(n, s, d, kind="ExternalOutput").ap()
    drt = lambda n, s, d: nc.dram_tensor(
        n, s, d, kind=("ExternalOutput" if debug else "Internal")).ap()

    x_own = dri("x_own", [T_OWN, H], F32)
    pmask_in = dri("pmask_in", [128, 8], F32)
    wq = dri("wq", [D, H], F32)
    wk = dri("wk", [D, H], F32)
    wv = dri("wv", [D, H], F32)
    w_o = dri("w_o", [H, D], F32)
    Mm = dri("Mm", [D, D], F32)
    zv = dri("zv", [D, 1], F32)
    gate = dri("gate", [1, 1], F32)
    gw = dri("gw", [II, H], I32)
    uw = dri("uw", [II, H], I32)
    dw = dri("dw", [H, II], I32)
    gs_in = dri("gs_in", [128, NIT], F32)
    us_in = dri("us_in", [128, NIT], F32)
    ds_in = dri("ds_in", [128, NHT], F32)
    maskT_in = dri("maskT_in", [128, 128], F32)
    ident_in = dri("ident_in", [128, 128], F32)
    ones_row_in = dri("ones_row_in", [1, 128], F32)
    neg128_row_in = dri("neg128_row_in", [1, 128], F32)

    y_out = dro("y", [T_OWN, H], F32)
    cc_in = nc.dram_tensor("cc_in", [128, 132], F32, kind="Internal").ap()
    cc_out = nc.dram_tensor("cc_out", [1024, 132], F32, kind="Internal").ap()
    xmid_sc = drt("xmid_sc", [T_OWN, H], F32)
    ffn_sc = drt("ffn_sc", [T_OWN, H], BF16)
    if debug:
        dbg_x2T = dro("dbg_x2T", [128, NHB * 1024], F32)
        dbg_hT = dro("dbg_hT", [128, 64 * 1024], F32)
        dbg_rsh = dro("dbg_rsh", [1, 1024], F32)
        dbg_qkv = dro("dbg_qkv", [128, 384], F32)
        dbg_pack = dro("dbg_pack", [128, 256], F32)
        dbg_outbf = dro("dbg_outbf", [128, 128], F32)
        dbg_sz = dro("dbg_sz", [128, 132], F32)
        dbg_tf = dro("dbg_tf", [128, 2048], F32)
        dbg_nd = dro("dbg_nd", [128, 8 * 264], F32)
        dbg_gath = dro("dbg_gath", [128, 1056], F32)
        dbg_pm = dro("dbg_pm", [128, 8], F32)
        dbg_xmb = dro("dbg_xmb", [128, 2048], F32)

    sched = {e: [] for e in ENGS}
    ctr = {}

    def em(eng, fn):
        sched[eng].append(fn)

    es = ExitStack()
    C = es.enter_context
    with es:
        sx = C(nc.semaphore("sx"))
        swc = C(nc.semaphore("swc"))
        stp = C(nc.semaphore("stp"))
        sst = C(nc.semaphore("sst"))
        pe = C(nc.semaphore("pe"))
        sv = C(nc.semaphore("sv"))
        ss = C(nc.semaphore("ss"))
        sg = C(nc.semaphore("sg"))
        SEM = {"sx": sx, "swc": swc, "stp": stp, "sst": sst, "pe": pe, "sv": sv,
               "ss": ss, "sg": sg}

        def getsem(name):
            if name not in SEM:
                SEM[name] = C(nc.semaphore(name))
            return SEM[name]

        def W(eng, sem_name, val=None):
            if isinstance(sem_name, tuple):
                sem_name, val = sem_name
            else:
                assert val is not None or sem_name is None, (eng, sem_name)
            if val is None or (isinstance(val, int) and val <= 0):
                return
            assert val <= ctr.get(sem_name, 0), (eng, sem_name, val, ctr.get(sem_name, 0))
            em(eng, lambda e, s=getsem(sem_name), v=val: e.wait_ge(s, v))

        def I(eng, fn, sem_name=None, by=1):
            if sem_name is None:
                em(eng, fn)
                return None
            ctr[sem_name] = ctr.get(sem_name, 0) + by
            v = ctr[sem_name]
            em(eng, lambda e, f=fn, s=getsem(sem_name), b=by: f(e).then_inc(s, b))
            return v

        def I2(eng, fn, slot):
            return (slot, I(eng, fn, slot, 16))

        ARENA_BYTES = 212800
        arena = C(nc.sbuf_tensor("arena", [128, ARENA_BYTES], U8))
        mem = Mem(nc, nc.lookup_mloc(arena).addr)

        o = [0]

        def P(nbytes, shape, dtype, align=64):
            off = (o[0] + align - 1) // align * align
            o[0] = off + nbytes
            return mem.at(off, shape, dtype)

        pmask = P(32, [128, 8], F32)
        ones_row = P(512, [1, 128], F32)
        neg128_row = P(512, [1, 128], F32)
        ident = P(512, [128, 128], F32)
        gs_sb = P(256, [128, NIT], F32)
        us_sb = P(256, [128, NIT], F32)
        ds_sb = P(64, [128, NHT], F32)
        epsln = P(4, [128, 1], F32)
        Sz = P(528, [128, 132], F32)
        Sz_bf = P(264, [128, 132], BF16)
        statsP = [P(136, [128, 34], F32) for _ in range(4)]
        aggrP = [P(16, [128, 4], F32) for _ in range(4)]
        rstdP = [P(4, [128, 1], F32) for _ in range(4)]
        nmrP = [P(4, [128, 1], F32) for _ in range(4)]
        lnvP = [P(4, [128, 1], F32) for _ in range(4)]
        sg_in = P(4, [1, 1], F32)
        g01 = P(8, [1, 2], F32)
        gcol = P(4, [128, 1], F32)
        gmcol = P(4, [128, 1], F32)
        ones_i = P(2, [128, 1], BF16)
        maskT_bf = P(256, [128, 128], BF16)
        Mz_bf = P(264, [128, 132], BF16)
        rsh_row = P(4096, [1, 1024], F32)
        x2T = P(NHB * 1024 * 2, [128, NHB * 1024], BF16)
        PA = (o[0] + 127) // 128 * 128

        def A(off, shape, dtype):
            return mem.at(PA + off, shape, dtype)

        NB = 4
        _aa = [0]

        def AA(nbytes, shape, dtype, align=64):
            off = (_aa[0] + align - 1) // align * align
            _aa[0] = off + nbytes
            return A(off, shape, dtype)

        wqkvT = AA(12288, [128, NHB * 384], BF16)
        xb = [AA(8192, [128, H], F32) for _ in range(3)]
        lnb = [AA(4096, [128, H], BF16) for _ in range(3)]
        ln1T = [AA(4096, [128, H], BF16) for _ in range(3)]

        qkv_sb = AA(1536, [128, 384], F32)
        kmin = AA(512, [128, 128], F32)
        qmin = AA(512, [128, 128], F32)
        ek = AA(512, [128, 128], F32)
        eq = AA(512, [128, 128], F32)
        pack = [AA(512, [128, 256], BF16) for _ in range(NB)]
        packT = [AA(512, [128, 256], BF16) for _ in range(8)]
        ndsv = AA(8448, [128, 8 * 264], F32)
        gath = AA(4224, [128, 1056], F32)
        Sp = AA(528, [128, 132], F32)
        Sp2 = AA(528, [128, 132], F32)
        vgdump = AA(4224, [128, 1056], F32)
        Sp_bf = AA(264, [128, 132], BF16)
        v1 = [AA(288, [128, 132], BF16) for _ in range(NB)]
        amt = [AA(256, [128, 128], BF16) for _ in range(NB)]
        t_loc = AA(512, [128, 128], F32)
        comb_scr = AA(64, [128, 8], F32)
        out_bf = [AA(256, [128, 128], BF16) for _ in range(NB)]
        outT = [AA(256, [128, 128], BF16) for _ in range(NB)]
        attnout = AA(8192, [128, H], F32)
        _attnout_off = _aa[0] - 8192
        wnat = A(_attnout_off, [128, H], BF16)
        t_ln = AA(8192, [128, H], F32)
        xmid = [AA(8192, [128, H], F32) for _ in range(2)]
        x2b = AA(4096, [128, H], BF16)

        GB = max((_aa[0] + 63) // 64 * 64, 131072)
        hT = A(0, [128, NIT * 1024], BF16)
        gn = A(GB, [128, H], BF16)
        un = A(GB + 4096, [128, H], BF16)
        gT = [A(GB + 8192, [128, H], BF16), A(GB + 12288, [128, H], BF16)]
        uT = [A(GB + 16384, [128, H], BF16), A(GB + 20480, [128, H], BF16)]
        sgb = A(GB + 24576, [128, 1024], F32)
        w_oT = A(GB + 24576, [128, H], BF16)

        dn = [A(GB + 4096 * i, [128, 2048], BF16) for i in range(3)]
        dT = [A(GB + 12288 + 4096 * i, [128, 2048], BF16) for i in range(3)]
        tdb = A(GB + 24576, [128, 1024], F32)
        stage = A(GB + 28672, [128, 1024], BF16)
        assert PA + GB + 32768 <= ARENA_BYTES, (PA, GB)

        xmb = [A(8192 * i, [128, H], F32) for i in range(4)]
        ffb = [A(32768 + 8192 * i, [128, H], F32) for i in range(4)]
        tfP = [A(65536 + 8192 * i, [128, H], F32) for i in range(4)]



        p_big = C(nc.psum_tensor("p_big", [128, 2048], F32))
        p_aux = C(nc.psum_tensor("p_aux", [128, 2048], F32))
        pm = p_aux[:, 0:132]
        pa_ = p_aux[:, 512:640]
        pnd = p_aux[:, 1024:1156]
        ps_ = p_aux[:, 1536:1668]
        prs0 = p_aux[0:1, 1024:1536]
        prs1 = p_aux[0:1, 1536:2048]

        _ln_rot = [0]
        _ln_last_norm = [0, 0, 0, 0]

        def ln_chain(src_fn, dst_fn):
            par = _ln_rot[0] & 3
            _ln_rot[0] += 1
            stats, aggr = statsP[par], aggrP[par]
            rstd, nmr, lnv = rstdP[par], nmrP[par], lnvP[par]

            def f_stats(v, g=src_fn):
                s_ = g()
                for b in range(3):
                    v.bn_stats(stats[:, 6 * b:6 * b + 6], s_[:, 512 * b:512 * (b + 1)])
                v.bn_stats(stats[:, 18:24], s_[:, 1536:2048])
                v.drain()
                v.reduce_sum(aggr[:, 0:1], stats[:, 1:24:3], axis=AX.X)
                v.tensor_tensor(stats[:, 24:32], stats[:, 1:24:3], stats[:, 1:24:3], ALU.mult)
                v.reduce_sum(aggr[:, 2:3], stats[:, 2:24:3], axis=AX.X)
                v.drain()
                v.tensor_scalar_mul(aggr[:, 0:1], aggr[:, 0:1], -0.125)
                v.reduce_sum(aggr[:, 3:4], stats[:, 24:32], axis=AX.X)
                v.drain()
                v.tensor_scalar(aggr[:, 3:4], aggr[:, 3:4], 256.0 / 2048.0, None, ALU.mult)
                v.tensor_scalar(aggr[:, 2:3], aggr[:, 2:3], 1.0 / 2048.0, None, ALU.mult)
                v.tensor_tensor(stats[:, 32:33], aggr[:, 0:1], aggr[:, 0:1], ALU.mult)
                v.drain()
                v.tensor_add(aggr[:, 1:2], aggr[:, 2:3], aggr[:, 3:4])
                v.drain()
                return v.tensor_sub(aggr[:, 1:2], aggr[:, 1:2], stats[:, 32:33])

            W("vector", "ss", _ln_last_norm[par])
            a = I("vector", f_stats, "sv")
            W("scalar", "sv", a)

            def f_scalar(s, sf=src_fn, df=dst_fn):
                s.drain()
                s.activation(lnv[:, :], aggr[:, 1:2], AF.Ln, bias=epsln[:, 0:1], scale=1.0)
                s.drain()
                s.activation(rstd[:, :], lnv[:, :], AF.Exp, bias=0.0, scale=-0.5)
                s.drain()
                s.activation(nmr[:, :], aggr[:, 0:1], AF.Identity, bias=0.0,
                             scale=rstd[:, 0:1])
                s.drain()
                return s.activation(df(), sf(), AF.Identity,
                                    bias=nmr[:, 0:1], scale=rstd[:, 0:1])
            r = I("scalar", f_scalar, "ss")
            _ln_last_norm[par] = r
            return r

        I("sync", lambda e: e.dma_start(out=pmask[:, :], in_=pmask_in[:, :]), "sx", 16)
        I("sync", lambda e: e.dma_start(out=ones_row[:, :], in_=ones_row_in[:, :]), "sx", 16)
        I("sync", lambda e: e.dma_start(out=neg128_row[:, :], in_=neg128_row_in[:, :]), "sx", 16)
        I("sync", lambda e: e.dma_start(out=ident[:, :], in_=ident_in[:, :]), "sx", 16)
        I("sync", lambda e: e.dma_start(out=gs_sb[:, :], in_=gs_in[:, :]), "sx", 16)
        I("sync", lambda e: e.dma_start(out=us_sb[:, :], in_=us_in[:, :]), "sx", 16)
        I("sync", lambda e: e.dma_start(out=ds_sb[:, :], in_=ds_in[:, :]), "sx", 16)
        sx_gate = I("sync", lambda e: e.dma_start(out=sg_in[:, :], in_=gate[:, :]), "sx", 16)
        W("vector", "sx", sx_gate)
        W("tensor", "sx", sx_gate)

        I("gpsimd", lambda e: e.dma_start(out=maskT_bf[:, :], in_=maskT_in[:, :]), "swcp", 16)
        I("gpsimd", lambda e: e.dma_start(out=Mz_bf[:, 0:128], in_=Mm[:, :]), "swcp", 16)
        swc_mz = I("gpsimd", lambda e: e.dma_start(out=Mz_bf[:, 128:129], in_=zv[:, :]), "swcp", 16)

        def vinit(v):
            v.memset(ones_i[:, :], 1.0)
            v.memset(epsln[:, :], LNEPS)
            v.memset(Sz[:, :], 0.0)
            v.memset(Sz_bf[:, :], 0.0)
            for vb in v1:
                v.memset(vb[:, 128:129], 1.0)
                v.memset(vb[:, 129:132], 0.0)
            return v.memset(Mz_bf[:, 129:132], 0.0)
        W("vector", "swcp", swc_mz)
        sv_init = I("vector", vinit, "sv")

        W("scalar", "sx", sx_gate)
        ss_eg = I("scalar", lambda e: e.activation(g01[0:1, 0:1], sg_in[0:1, 0:1], AF.Exp,
                                                   bias=0.0, scale=-1.0), "ss")
        W("vector", "ss", ss_eg)

        def vgate(v):
            v.tensor_scalar_add(g01[0:1, 0:1], g01[0:1, 0:1], 1.0)
            v.drain()
            return v.reciprocal(g01[0:1, 1:2], g01[0:1, 0:1])
        sv_g = I("vector", vgate, "sv")

        W("tensor", "sv", sv_g)
        pe_g = I("tensor", lambda e: e.matmul(p_aux[:, 0:1], ones_row[0:1, :],
                                              g01[0:1, 1:2], start=True, stop=True), "pe")
        W("vector", "pe", pe_g)

        def vgate2(v):
            v.tensor_copy(gcol[:, :], p_aux[:, 0:1])
            v.drain()
            return v.tensor_scalar(gmcol[:, :], gcol[:, :], -1.0, 1.0, ALU.mult, ALU.add)
        sv_g2 = I("vector", vgate2, "sv")

        def tp_qkvw(e, off):
            in3 = wnat[:, :].rearrange("p (b j) -> p b j", b=NHB)
            out3 = wqkvT[:, :].rearrange("q (b i) -> q b i", i=384)[:, :, off:off + 128]
            return e.dma_start(out=out3, in_=in3, transpose=True)

        cwq = I("gpsimd", lambda e: e.dma_start(out=wnat[:, :], in_=wq[:, :]), "swn", 16)
        W("sync", "swn", cwq)
        t1_ = I("sync", lambda e: tp_qkvw(e, 0), "stw", 16)
        W("gpsimd", "stw", t1_)
        cwk = I("gpsimd", lambda e: e.dma_start(out=wnat[:, :], in_=wk[:, :]), "swn", 16)
        W("sync", "swn", cwk)
        t2_ = I("sync", lambda e: tp_qkvw(e, 128), "stw", 16)
        W("gpsimd", "stw", t2_)
        cwv = I("gpsimd", lambda e: e.dma_start(out=wnat[:, :], in_=wv[:, :]), "swn", 16)
        W("sync", "swn", cwv)
        tp_wv = I("sync", lambda e: tp_qkvw(e, 256), "stw", 16)
        W("gpsimd", "stw", tp_wv)
        cwo = I("gpsimd", lambda e: e.dma_start(
            out=wnat[:, :].rearrange("p (b d) -> p b d", b=NHB),
            in_=w_o.rearrange("(b p) d -> p b d", p=128)), "swn", 16)
        W("sync", "swn", cwo)
        tp_wo = I("sync", lambda e: e.dma_start(
            out=w_oT[:, :].rearrange("q (b i) -> q b i", b=NHB),
            in_=wnat[:, :].rearrange("p (b j) -> p b j", b=NHB),
            transpose=True), "stw", 16)

        CG, CU, TG, TU = {}, {}, {}, {}
        PG, PU, SG, HT = {}, {}, {}, {}

        def emit_gate_cast(it):
            if it >= 1:
                W("gpsimd", TG[it - 1])
            CG[it] = I2("gpsimd", lambda e, i=it: e.dma_start(
                out=gn[:, :], in_=gw[i * 128:(i + 1) * 128, :]), "scg")
            if it >= 1:
                W("gpsimd", TU[it - 1])
            CU[it] = I2("gpsimd", lambda e, i=it: e.dma_start(
                out=un[:, :], in_=uw[i * 128:(i + 1) * 128, :]), "scu")

        def emit_gate_tp(it):
            b = it % 2
            W("sync", CG[it])
            if it >= 2:
                W("sync", "pe", PG[it - 2])
            TG[it] = I2("sync", lambda e, bb=b: e.dma_start(
                out=gT[bb][:, :].rearrange("q (b2 i) -> q b2 i", b2=NHB),
                in_=gn[:, :].rearrange("p (b2 j) -> p b2 j", b2=NHB),
                transpose=True), "stg%d" % b)
            W("sync", CU[it])
            if it >= 2:
                W("sync", "pe", PU[it - 2])
            TU[it] = I2("sync", lambda e, bb=b: e.dma_start(
                out=uT[bb][:, :].rearrange("q (b2 i) -> q b2 i", b2=NHB),
                in_=un[:, :].rearrange("p (b2 j) -> p b2 j", b2=NHB),
                transpose=True), "stu%d" % b)

        ln_done, qkvfree, st_done, su_done = {}, {}, {}, {}
        xmid_done, xmid_store, x2_tp, x2n_done = {}, {}, {}, {}
        evacA_ss, evacB_ss, post_norm = {}, {}, {}
        tpl, tpp, QK, ND, EX, PKV, AM, OB, TO = {}, {}, {}, {}, {}, {}, {}, {}, {}
        PM, PA_, WO1, WO2, XR, NSV, NDF = {}, {}, {}, {}, {}, {}, {}

        pmnd = [p_big[:, 512:776], p_aux[:, 0:264]]

        def a0_load(t):
            if t >= 3:
                W("sync", "ss", ln_done[t - 3])
            lx = I2("sync", lambda e, q=t, d=xb[t % 3]: e.dma_start(
                out=d[:, :], in_=x_own[q * 128:(q + 1) * 128, :]), "sxb%d" % (t % 3))
            ln_done[("lx", t)] = lx

        def a1_ln(t):
            W("vector", ln_done[("lx", t)])
            if t == 0:
                W("vector", "sv", sv_g2)
            ln_done[t] = ln_chain(lambda b=t % 3: xb[b][:, :], lambda b=t % 3: lnb[b][:, :])

        def a2_tln(t):
            W("sync", "ss", ln_done[t])
            if t >= 3:
                W("sync", "pe", QK[t - 3])
            tpl[t] = I2("sync", lambda e, b=t % 3: e.dma_start(
                out=ln1T[b][:, :].rearrange("q (b2 i) -> q b2 i", b2=NHB),
                in_=lnb[b][:, :].rearrange("p (b2 j) -> p b2 j", b2=NHB),
                transpose=True), "stl%d" % (t % 3))

        def a3_qkv(t):
            W("tensor", tpl[t])
            if t == 0:
                W("tensor", "stw", tp_wv)
                W("tensor", "stw", tp_wo)
            if t >= 1:
                W("tensor", "sv", qkvfree[t - 1])

            def qkv_mm(e, b=t % 3):
                mm = None
                for hb in range(NHB):
                    mm = e.matmul(p_big[:, 0:384], ln1T[b][:, hb * 128:(hb + 1) * 128],
                                  wqkvT[:, hb * 384: hb * 384 + 384],
                                  start=(hb == 0), stop=(hb == NHB - 1))
                return mm
            QK[t] = I("tensor", qkv_mm, "pe")

        def a4_evac(t):
            W("vector", "pe", QK[t])
            if t >= 1:
                W("vector", "ss", EX[t - 1])

            def ev_sig(v):
                v.drain()
                v.tensor_copy(qkv_sb[:, 0:384], p_big[:, 0:384])
                v.drain()
                v.tensor_scalar_min(kmin[:, :], qkv_sb[:, 128:256], 0.0)
                return v.tensor_scalar_min(qmin[:, :], qkv_sb[:, 0:128], 0.0)
            qkvfree[t] = I("vector", ev_sig, "sv")
            if debug and t == 0:
                W("gpsimd", "sv", qkvfree[t])
                I("gpsimd", lambda e: e.dma_start(out=dbg_qkv[:, :], in_=qkv_sb[:, :]), "sdbg", 16)

            W("scalar", "sv", qkvfree[t])
            if t >= 1:
                W("scalar", "sv", PKV[t - 1])

            def s_exp(s):
                s.drain()
                s.activation(ek[:, :], kmin[:, :], AF.Exp, bias=0.0, scale=1.0)
                s.drain()
                return s.activation(eq[:, :], qmin[:, :], AF.Exp, bias=0.0, scale=1.0)
            EX[t] = I("scalar", s_exp, "ss")

        def a5_sig(t):
            W("vector", "ss", EX[t])
            if t >= 4:
                W("vector", "pe", st_done[t - 4])

            def v_sig(v, b=t % 4):
                v.drain()
                v.scalar_tensor_tensor(pack[b][:, 128:256], qkv_sb[:, 128:256],
                                       1.0, ek[:, :], ALU.add, ALU.max)
                v.scalar_tensor_tensor(pack[b][:, 0:128], qkv_sb[:, 0:128],
                                       1.0, eq[:, :], ALU.add, ALU.max)
                return v.tensor_copy(v1[b][:, 0:128], qkv_sb[:, 256:384])
            PKV[t] = I("vector", v_sig, "sv")
            if debug and t == 0:
                W("gpsimd", "sv", PKV[t])
                I("gpsimd", lambda e: e.dma_start(out=dbg_pack[:, :], in_=pack[t % 4][:, :]), "sdbg", 16)

        def a6_tpack(t):
            W("sync", "sv", PKV[t])
            tpp[t] = I2("sync", lambda e, b4=t % 4, b8=t: e.dma_start(
                out=packT[b8][:, :].rearrange("q (b2 i) -> q b2 i", b2=2),
                in_=pack[b4][:, :].rearrange("p (b2 j) -> p b2 j", b2=2),
                transpose=True), "stpp%d" % (t % 4))

        def a7_ma(t):
            par = t % 2
            W("tensor", tpp[t])
            if t >= 2:
                W("tensor", "sv", NSV[t - 2])
            PM[t] = I("tensor", lambda e, b=t, p=par: e.matmul(
                pmnd[p][:, 0:132], packT[b][:, 0:128], Mz_bf[:, 0:132],
                start=True, stop=True), "pe")
            if t >= 1:
                W("tensor", "sv", AM[t - 1])
            PA_[t] = I("tensor", lambda e, b=t: e.matmul(
                pa_, packT[b][:, 128:256], packT[b][:, 0:128],
                start=True, stop=True), "pe")
            W("vector", "pe", PA_[t])
            AM[t] = I("vector", lambda v, b=t % 4: v.tensor_mul(
                amt[b][:, :], pa_, maskT_bf[:, :]), "sv")

        def a8_ndst(t):
            par = t % 2
            W("tensor", "sv", AM[t])
            W("tensor", "sv", su_done[t - 1] if t >= 1 else sv_init)

            def nd_mm(e, b4=t % 4, b8=t, p=par):
                e.matmul(pmnd[p][:, 132:264], amt[b4][:, :], v1[b4][:, 0:132],
                         start=True, stop=False)
                return e.matmul(pmnd[p][:, 132:264], packT[b8][:, 0:128],
                                Sz_bf[:, 0:132], start=False, stop=True)
            ND[t] = I("tensor", nd_mm, "pe")

            W("tensor", "sv", PKV[t])
            st_done[t] = I("tensor", lambda e, b4=t % 4: e.matmul(
                ps_, pack[b4][:, 128:256], v1[b4][:, 0:132],
                start=True, stop=True), "pe")

            W("vector", "pe", st_done[t])

            def v_su(v):
                v.drain()
                v.tensor_add(Sz[:, :], Sz[:, :], ps_)
                v.drain()
                return v.tensor_copy(Sz_bf[:, :], Sz[:, :])
            su_done[t] = I("vector", v_su, "sv")

        def a9_nsv(t):
            par = t % 2
            W("vector", "pe", ND[t])
            NSV[t] = I("vector", lambda v, p=par, t_=t: (v.drain(), v.tensor_copy(
                ndsv[:, t_ * 264:(t_ + 1) * 264], pmnd[p]))[1], "sv")

        A_STAGES = [a0_load, a1_ln, a2_tln, a3_qkv, a4_evac, a5_sig, a6_tpack,
                    a7_ma, a8_ndst, a9_nsv]
        NA = len(A_STAGES)
        for s in range(NT_OWN + NA):
            for k in range(NA - 1, -1, -1):
                t = s - k
                if 0 <= t < NT_OWN:
                    A_STAGES[k](t)
            if s == 6:
                emit_gate_cast(0)
                emit_gate_tp(0)
            if s == 11:
                emit_gate_cast(1)
                emit_gate_tp(1)

        W("gpsimd", "sv", su_done[NT_OWN - 1])
        ccs = I2("gpsimd", lambda e: e.dma_start(out=cc_in[:, :], in_=Sz[:, :]), "scci")
        W("gpsimd", ccs)
        ccd = I("gpsimd", lambda e: e.collective_compute(
            "AllGather", ALU.bypass, replica_groups=[list(range(8))],
            ins=[cc_in.opt()], outs=[cc_out.opt()]), "scc", 1)
        W("gpsimd", "scc", ccd)
        ccl = I2("gpsimd", lambda e: e.dma_start(
            out=gath[:, :].rearrange("p (r c) -> p r c", c=132),
            in_=cc_out.rearrange("(r p) c -> p r c", p=128)), "sccl")

        def v_prefix(v):
            v.memset(Sp[:, :], 0.0)
            v.drain()
            if debug:
                v.tensor_copy(vgdump[:, :], gath[:, :])
                v.drain()
            for r in range(8):
                v.tensor_scalar(Sp2[:, :], gath[:, r * 132:(r + 1) * 132],
                                pmask[:, r:r + 1], None, ALU.mult)
                v.drain()
                v.tensor_add(Sp[:, :], Sp[:, :], Sp2[:, :])
                v.drain()
            return v.tensor_copy(Sp_bf[:, :], Sp[:, :])
        W("vector", ccl)
        spv = I("vector", v_prefix, "sv")
        if debug:
            W("gpsimd", "sv", spv)
            I("gpsimd", lambda e: e.dma_start(out=dbg_sz[:, :], in_=Sp[:, :]), "sdbg", 16)
            W("gpsimd", "sv", NSV[NT_OWN - 1])
            I("gpsimd", lambda e: e.dma_start(out=dbg_nd[:, :], in_=ndsv[:, :]), "sdbg", 16)
            I("gpsimd", lambda e: e.dma_start(out=dbg_gath[:, :], in_=vgdump[:, :]), "sdbg", 16)
            I("gpsimd", lambda e: e.dma_start(out=dbg_pm[:, :], in_=pmask[:, :]), "sdbg", 16)

        def b0_ndfix(t):
            par = t % 2
            if t == 0:
                W("tensor", "sv", spv)
            W("tensor", "sv", NSV.get(t + 6) if t < 2 else OB[t - 2])

            def f(e, b8=t, p=par):
                return e.matmul(pmnd[p][:, 0:132], packT[b8][:, 0:128],
                                Sp_bf[:, 0:132], start=True, stop=True)
            NDF[t] = I("tensor", f, "pe")

        def b1_comb(t):
            par = t % 2
            W("vector", "pe", NDF[t])

            def v_comb(v, b=t % 4, p=par, t_=t):
                nb = ndsv[:, t_ * 264:(t_ + 1) * 264]
                v.drain()
                v.tensor_scalar_add(comb_scr[:, 0:1], nb[:, 128:129], AEPS)
                v.scalar_tensor_tensor(comb_scr[:, 1:2], nb[:, 260:261], AEPS,
                                       pmnd[p][:, 128:129], ALU.add, ALU.add)
                v.drain()
                v.reciprocal(comb_scr[:, 4:5], comb_scr[:, 0:1])
                v.reciprocal(comb_scr[:, 5:6], comb_scr[:, 1:2])
                v.drain()
                v.tensor_mul(comb_scr[:, 2:3], comb_scr[:, 4:5], gcol[:, :])
                v.tensor_mul(comb_scr[:, 3:4], comb_scr[:, 5:6], gmcol[:, :])
                v.drain()
                v.tensor_add(t_loc[:, :], nb[:, 132:260], pmnd[p][:, 0:128])
                v.drain()
                v.tensor_scalar(t_loc[:, :], t_loc[:, :], comb_scr[:, 3:4], None, ALU.mult)
                v.drain()
                return v.scalar_tensor_tensor(out_bf[b][:, :], nb[:, 0:128],
                                              comb_scr[:, 2:3], t_loc[:, :],
                                              ALU.mult, ALU.add)
            OB[t] = I("vector", v_comb, "sv")
            if debug and t == 0:
                W("gpsimd", "sv", OB[t])
                I("gpsimd", lambda e: e.dma_start(out=dbg_outbf[:, :], in_=out_bf[0][:, :]), "sdbg", 16)

            W("sync", "sv", OB[t])
            TO[t] = I2("sync", lambda e, b=t % 4: e.dma_start(
                out=outT[b][:, :], in_=out_bf[b][:, :], transpose=True), "sto%d" % (t % 4))

        def b2_wo1(t):
            W("tensor", TO[t])
            if t >= 1:
                W("tensor", "ss", evacB_ss.get(t - 1))

            def wo1(e, b=t % 4):
                e.matmul(p_big[:, 1024:1536], outT[b][:, :], w_oT[:, 0:512],
                         start=True, stop=True)
                return e.matmul(p_big[:, 1536:2048], outT[b][:, :], w_oT[:, 512:1024],
                                start=True, stop=True)
            WO1[t] = I("tensor", wo1, "pe")
            W("scalar", "pe", WO1[t])
            evacA_ss[t] = I("scalar", lambda s: (s.drain(), s.activation(
                attnout[:, 0:1024], p_big[:, 1024:2048], AF.Identity,
                bias=0.0, scale=1.0))[1], "ss")
            if t == 0:
                W("sync", "ss", ln_done[NT_OWN - 1])
            if t >= 3:
                W("sync", "sg", xmid_done[t - 3])
            XR[t] = I2("sync", lambda e, q=t: e.dma_start(
                out=xb[q % 3][:, :], in_=x_own[q * 128:(q + 1) * 128, :]),
                "sxb%d" % (t % 3))

        def b3_wo2(t):
            W("tensor", "ss", evacA_ss[t])

            def wo2(e, b=t % 4):
                e.matmul(p_big[:, 1024:1536], outT[b][:, :], w_oT[:, 1024:1536],
                         start=True, stop=True)
                return e.matmul(p_big[:, 1536:2048], outT[b][:, :], w_oT[:, 1536:2048],
                                start=True, stop=True)
            WO2[t] = I("tensor", wo2, "pe")
            W("scalar", "pe", WO2[t])
            evacB_ss[t] = I("scalar", lambda s: (s.drain(), s.activation(
                attnout[:, 1024:2048], p_big[:, 1024:2048], AF.Identity,
                bias=0.0, scale=1.0))[1], "ss")
            W("vector", "ss", evacB_ss[t])
            if t >= 1:
                W("scalar", "sg", xmid_done[t - 1])
            post_norm[t] = ln_chain(lambda: attnout[:, :], lambda: t_ln[:, :])

        def b4_xm(t):
            W("gpsimd", "ss", post_norm[t])
            W("gpsimd", XR[t])
            if t >= 2:
                W("gpsimd", xmid_store[t - 2])
                W("gpsimd", "ss", x2n_done[t - 2])
            xm = I("gpsimd", lambda g, xp=t % 2, q=t: g.tensor_add(
                xmid[xp][:, :], xb[q % 3][:, :], t_ln[:, :]), "sg")
            xmid_done[t] = xm
            W("sync", "sg", xm)
            xmid_store[t] = I2("sync", lambda e, xp=t % 2, q=t: e.dma_start(
                out=xmid_sc[q * 128:(q + 1) * 128, :], in_=xmid[xp][:, :]), "sxs%d" % (t % 2))

        def b5_ln2(t):
            if t >= 1:
                W("scalar", x2_tp[t - 1])
            W("vector", "sg", xmid_done[t])
            x2n_done[t] = ln_chain(lambda xp=t % 2: xmid[xp][:, :], lambda: x2b[:, :])

        def b6_tx2(t):
            W("sync", "ss", x2n_done[t])
            x2_tp[t] = I2("sync", lambda e, q=t: e.dma_start(
                out=x2T[:, :].rearrange("q2 (b2 t2) -> q2 b2 t2", b2=NHB)[:, :, q * 128:(q + 1) * 128],
                in_=x2b[:, :].rearrange("p (b2 j) -> p b2 j", b2=NHB),
                transpose=True), "sx2%d" % (t % 2))

        B_STAGES = [b0_ndfix, b1_comb, b2_wo1, b3_wo2, b4_xm, b5_ln2, b6_tx2]
        NBST = len(B_STAGES)
        for s in range(NT_OWN + NBST):
            for k in range(NBST - 1, -1, -1):
                t = s - k
                if 0 <= t < NT_OWN:
                    B_STAGES[k](t)

        for it in range(NIT):
            if it >= 2:
                emit_gate_cast(it)
                emit_gate_tp(it)
            b = it % 2

            W("tensor", TG[it])
            if it == 0:
                W("tensor", x2_tp[NT_OWN - 1])
            if it >= 1:
                W("tensor", "ss", SG[it - 1])

            def g_mm(e, bb=b):
                mm = None
                for hb in range(NHB):
                    for hf in range(2):
                        mm = e.matmul(p_big[:, hf * 512:(hf + 1) * 512],
                                      gT[bb][:, hb * 128:(hb + 1) * 128],
                                      x2T[:, hb * 1024 + hf * 512: hb * 1024 + (hf + 1) * 512],
                                      start=(hb == 0), stop=(hb == NHB - 1))
                return mm
            PG[it] = I("tensor", g_mm, "pe")

            W("tensor", TU[it])
            if it >= 1:
                W("tensor", "sv", HT[it - 1])

            def u_mm(e, bb=b):
                mm = None
                for hb in range(NHB):
                    for hf in range(2):
                        mm = e.matmul(p_big[:, 1024 + hf * 512:1024 + (hf + 1) * 512],
                                      uT[bb][:, hb * 128:(hb + 1) * 128],
                                      x2T[:, hb * 1024 + hf * 512: hb * 1024 + (hf + 1) * 512],
                                      start=(hb == 0), stop=(hb == NHB - 1))
                return mm
            PU[it] = I("tensor", u_mm, "pe")

            if it >= 1:
                def rs_mm(e, i=it - 1):
                    e.matmul(prs0, ones_i[:, 0:1], hT[:, i * 1024:i * 1024 + 512],
                             start=(i == 0), stop=False)
                    return e.matmul(prs1, ones_i[:, 0:1], hT[:, i * 1024 + 512:(i + 1) * 1024],
                                    start=(i == 0), stop=False)
                I("tensor", rs_mm, "pe")

            W("scalar", "pe", PG[it])
            if it >= 1:
                W("scalar", "sv", HT[it - 1])

            def s_silu(s, i=it):
                s.drain()
                return s.activation(sgb[:, :], p_big[:, 0:1024], AF.Silu,
                                    bias=0.0, scale=gs_sb[:, i:i + 1])
            SG[it] = I("scalar", s_silu, "ss")

            W("vector", "pe", PU[it])
            W("vector", "ss", SG[it])
            if it == 0:
                W("vector", xmid_store[NT_OWN - 1])
                W("vector", xmid_store[NT_OWN - 2])

            def v_h(v, i=it):
                v.drain()
                return v.scalar_tensor_tensor(hT[:, i * 1024:(i + 1) * 1024],
                                              p_big[:, 1024:2048], us_sb[:, i:i + 1],
                                              sgb[:, :], ALU.mult, ALU.mult)
            HT[it] = I("vector", v_h, "sv")

        if debug:
            W("gpsimd", x2_tp[NT_OWN - 1])
            I("gpsimd", lambda e: e.dma_start(out=dbg_x2T[:, :], in_=x2T[:, :]), "sdbg", 16)
            W("gpsimd", "sv", HT[63])
            I("gpsimd", lambda e: e.dma_start(out=dbg_hT[:, :], in_=hT[:, :]), "sdbg", 16)

        def rs_fin(e):
            e.matmul(prs0, ones_i[:, 0:1], hT[:, 63 * 1024:63 * 1024 + 512],
                     start=False, stop=True)
            return e.matmul(prs1, ones_i[:, 0:1], hT[:, 63 * 1024 + 512:64 * 1024],
                            start=False, stop=True)
        W("tensor", "sv", HT[63])
        rsf = I("tensor", rs_fin, "pe")
        W("vector", "pe", rsf)
        rsv = I("vector", lambda v: v.tensor_copy(rsh_row[0:1, :], p_aux[0:1, 1024:2048]), "sv")

        if debug:
            W("gpsimd", "sv", rsv)
            I("gpsimd", lambda e: e.dma_start(out=dbg_rsh[:, :], in_=rsh_row[0:1, :]), "sdbg", 16)

        CD, TD, PD, TDv, DQ = {}, {}, {}, {}, {}
        SF, TRC = {}, {}

        def emit_down_cast(k):
            ht, q = divmod(k, 4)
            bq = k % 3
            if k <= 2:
                W("gpsimd", TU[NIT - 1])
                W("gpsimd", "pe", PU[NIT - 1])
            if k >= 3:
                W("gpsimd", TD[k - 3])
            CD[k] = I2("gpsimd", lambda e, h=ht, qq=q: e.dma_start(
                out=dn[bq][:, :], in_=dw[h * 128:(h + 1) * 128, qq * 2048:(qq + 1) * 2048]),
                "scd%d" % bq)

        def emit_down_tp(k):
            bq = k % 3
            W("sync", CD[k])
            if k <= 2:
                W("sync", "pe", PU[NIT - 1])
            if k >= 3:
                W("sync", "pe", DQ[k - 3])
            TD[k] = I2("sync", lambda e: e.dma_start(
                out=dT[bq][:, :].rearrange("q2 (b2 i) -> q2 b2 i", b2=NHB),
                in_=dn[bq][:, :].rearrange("p (b2 j) -> p b2 j", b2=NHB),
                transpose=True), "std%d" % bq)

        for k in range(3):
            emit_down_cast(k)
            emit_down_tp(k)

        pdb = [p_big[:, 0:1024], p_aux[:, 1024:2048]]
        for ht in range(NHT):
            dpar = ht % 2
            if ht >= 2:
                W("tensor", "sv", TDv[ht - 2])
            elif ht == 0:
                W("tensor", "ss", SG[NIT - 1])
                W("tensor", "sv", rsv)
            else:
                W("tensor", "sv", rsv)

            for q in range(4):
                k = ht * 4 + q
                W("tensor", TD[k])

                def d_mm(e, qq=q, bq=k % 3, dp=dpar):
                    mm = None
                    for j in range(NHB):
                        ib = qq * 16 + j
                        for hf in range(2):
                            mm = e.matmul(pdb[dp][:, hf * 512:(hf + 1) * 512],
                                          dT[bq][:, j * 128:(j + 1) * 128],
                                          hT[:, ib * 1024 + hf * 512: ib * 1024 + (hf + 1) * 512],
                                          start=(ib == 0), stop=False)
                    return mm
                DQ[k] = I("tensor", d_mm, "pe")
                if k + 3 < NHT * 4:
                    emit_down_cast(k + 3)
                    emit_down_tp(k + 3)

            def d_corr(e, dp=dpar):
                e.matmul(pdb[dp][:, 0:512], neg128_row[0:1, :], rsh_row[0:1, 0:512],
                         start=False, stop=True)
                return e.matmul(pdb[dp][:, 512:1024], neg128_row[0:1, :],
                                rsh_row[0:1, 512:1024], start=False, stop=True)
            PD[ht] = I("tensor", d_corr, "pe")

            W("vector", "pe", PD[ht])
            if ht >= 1:
                W("vector", "pe", TRC[ht - 1])

            def v_td(v, h=ht, dp=dpar):
                v.drain()
                return v.tensor_scalar(tdb[:, :], pdb[dp][:, 0:1024], ds_sb[:, h:h + 1],
                                       None, ALU.mult)
            TDv[ht] = I("vector", v_td, "sv")

            W("tensor", "sv", TDv[ht])
            if ht >= 1:
                W("tensor", SF[ht - 1])
            cps = []
            for j in range(8):
                scr = p_big[:, 1024 + (j % 2) * 512: 1024 + (j % 2) * 512 + 128]
                if j >= 2:
                    W("tensor", "sv", cps[j - 2])
                trc = I("tensor", lambda e, s=scr, j_=j, h=ht: e.transpose(
                    s, tdb[:, j_ * 128:(j_ + 1) * 128], ident[:, :]), "pe")
                if j == 7:
                    TRC[ht] = trc
                W("vector", "pe", trc)
                cp = I("vector", lambda v, s=scr, j_=j: (v.drain(), v.tensor_copy(
                    stage[:, j_ * 128:(j_ + 1) * 128], s))[1], "sv")
                cps.append(cp)

            W("sync", "sv", cps[-1])
            SF[ht] = I2("sync", lambda e, h=ht: e.dma_start(
                out=ffn_sc.rearrange("(t2 p) (h2 j) -> p t2 h2 j", p=128, j=128)[:, :, h, :],
                in_=stage[:, :].rearrange("p (t2 j) -> p t2 j", j=128)), "ssf")

        YS, Yv, LX1, LX2 = {}, {}, {}, {}

        def p4_load(tt):
            bb = tt % 4
            if tt >= 4:
                W("sync", YS[tt - 4])
                W("sync", "sg", Yv[tt - 4])
                W("gpsimd", YS[tt - 4])
                W("gpsimd", "sg", Yv[tt - 4])
            LX1[tt] = I2("sync", lambda e, q=tt, b2=bb: e.dma_start(
                out=xmb[b2][:, :], in_=xmid_sc[q * 128:(q + 1) * 128, :]), "sxl%d" % bb)
            LX2[tt] = I2("gpsimd", lambda e, q=tt, b2=bb: e.dma_start(
                out=ffb[b2][:, :], in_=ffn_sc[q * 128:(q + 1) * 128, :]), "sfb%d" % bb)

        W("sync", "pe", TRC[NHT - 1])
        W("sync", SF[NHT - 1])
        W("gpsimd", "pe", TRC[NHT - 1])
        W("gpsimd", SF[NHT - 1])
        for tt in range(4):
            p4_load(tt)
        for tt in range(NT_OWN):
            b = tt % 4
            W("vector", LX2[tt])
            if tt >= 4:
                W("scalar", "sg", Yv[tt - 4])
            ffn_ln = ln_chain(lambda bb=b: ffb[bb][:, :], lambda bb=b: tfP[bb][:, :])
            W("gpsimd", "ss", ffn_ln)
            W("gpsimd", LX1[tt])

            def v_y(g, bb=b):
                return g.tensor_add(ffb[bb][:, :], xmb[bb][:, :], tfP[bb][:, :])
            Yv[tt] = I("gpsimd", v_y, "sg")
            if debug and tt == 0:
                W("gpsimd", "ss", ffn_ln)
                I("gpsimd", lambda e: e.dma_start(out=dbg_tf[:, :], in_=tfP[0][:, :]), "sdbg", 16)
                I("gpsimd", lambda e: e.dma_start(out=dbg_xmb[:, :], in_=xmb[0][:, :]), "sdbg", 16)

            W("sync", "sg", Yv[tt])
            YS[tt] = I2("sync", lambda e, q=tt, bb=b: e.dma_start(
                out=y_out[q * 128:(q + 1) * 128, :], in_=ffb[bb][:, :]), "sys%d" % b)
            if tt + 4 < NT_OWN:
                p4_load(tt + 4)

        for tt in range(NT_OWN - 4, NT_OWN):
            W("sync", YS[tt])

        block = C(nc.Block())

        def run(lst):
            def f(e):
                for fn in lst:
                    fn(e)
            return f
        block.sync(run(sched["sync"]))
        block.gpsimd(run(sched["gpsimd"]))
        block.scalar(run(sched["scalar"]))
        block.vector(run(sched["vector"]))
        block.tensor(run(sched["tensor"]))

    return nc


_NC_CACHE = None


def _get_nc():
    global _NC_CACHE
    if _NC_CACHE is None:
        _NC_CACHE = build_program()
    return _NC_CACHE


def kernel(**inputs):
    X = np.asarray(inputs["L0_attn_hidden_states"], dtype=np.float32)
    wq = np.ascontiguousarray(inputs["L0_attn_w_q"], dtype=np.float32)
    wk = np.ascontiguousarray(inputs["L0_attn_w_k"], dtype=np.float32)
    wv = np.ascontiguousarray(inputs["L0_attn_w_v"], dtype=np.float32)
    w_o = np.ascontiguousarray(inputs["L0_attn_w_o"], dtype=np.float32)
    gate = np.asarray(inputs["L0_attn_gate"], dtype=np.float32).reshape(1, 1)
    Mm = np.ascontiguousarray(inputs["L0_attn_M"], dtype=np.float32)
    zv = np.asarray(inputs["L0_attn_z"], dtype=np.float32).reshape(128, 1)
    gw = np.ascontiguousarray(inputs["L0_ffn_gate_w"], dtype=np.int32)
    uw = np.ascontiguousarray(inputs["L0_ffn_up_w"], dtype=np.int32)
    dw = np.ascontiguousarray(inputs["L0_ffn_down_w"], dtype=np.int32)
    gs = np.ascontiguousarray(np.asarray(inputs["L0_ffn_gate_scale"], dtype=np.float32)
                              .reshape(NIT, 128).T)
    us = np.ascontiguousarray(np.asarray(inputs["L0_ffn_up_scale"], dtype=np.float32)
                              .reshape(NIT, 128).T)
    ds = np.ascontiguousarray(np.asarray(inputs["L0_ffn_down_scale"], dtype=np.float32)
                              .reshape(NHT, 128).T)

    maskT = np.triu(np.ones((128, 128), dtype=np.float32))
    ident = np.eye(128, dtype=np.float32)
    ones_row = np.ones((1, 128), dtype=np.float32)
    neg128_row = np.full((1, 128), -128.0, dtype=np.float32)

    in_maps = []
    for c in range(8):
        bb, j = divmod(c, 4)
        pm = np.zeros((128, 8), dtype=np.float32)
        pm[:, bb * 4: bb * 4 + j] = 1.0
        xo = np.ascontiguousarray(X[bb, j * 1024:(j + 1) * 1024])
        in_maps.append({
            "x_own": xo, "pmask_in": pm,
            "wq": wq, "wk": wk, "wv": wv, "w_o": w_o, "Mm": Mm, "zv": zv,
            "gate": gate, "gw": gw, "uw": uw, "dw": dw,
            "gs_in": gs, "us_in": us, "ds_in": ds,
            "maskT_in": maskT, "ident_in": ident,
            "ones_row_in": ones_row, "neg128_row_in": neg128_row,
        })

    nc = _get_nc()
    res = run_bass_kernel_spmd(nc, in_maps, core_ids=list(range(8)))
    out = np.empty((2, 4096, H), dtype=np.float32)
    for c in range(8):
        bb, j = divmod(c, 4)
        out[bb, j * 1024:(j + 1) * 1024] = res.results[c]["y"]
    return out


if __name__ == "__main__":
    build_program()
    print("build ok")


# version 16
# speedup vs baseline: 1.0759x; 1.0759x over previous
import numpy as np
from contextlib import ExitStack

import concourse.bass as bass
import concourse.mybir as mybir
from concourse.bass_utils import run_bass_kernel_spmd

F32 = mybir.dt.float32
BF16 = mybir.dt.bfloat16
I32 = mybir.dt.int32
U8 = mybir.dt.uint8
AF = mybir.ActivationFunctionType
ALU = mybir.AluOpType
AX = mybir.AxisListType

H = 2048
D = 128
II = 8192
NHB = 16
NIT = 64
NHT = 16
T_OWN = 1024
NT_OWN = 8
NT_HIST = 24
NT = NT_HIST + NT_OWN
LNEPS = 1e-5
AEPS = 1e-6

ENGS = ("sync", "gpsimd", "scalar", "vector", "tensor")


class Mem:

    def __init__(self, nc, base):
        self.nc = nc
        self.base = base
        self.n = 0

    def at(self, off, shape, dtype):
        self.n += 1
        h = self.nc.alloc_sbuf_tensor_at(f"m{self.n}", shape, dtype,
                                         offset=self.base + off)
        return h.ap()


def build_program(debug=False):
    nc = bass.Bass("TRN2", target_bir_lowering=False)

    dri = lambda n, s, d: nc.dram_tensor(n, s, d, kind="ExternalInput").ap()
    dro = lambda n, s, d: nc.dram_tensor(n, s, d, kind="ExternalOutput").ap()
    drt = lambda n, s, d: nc.dram_tensor(
        n, s, d, kind=("ExternalOutput" if debug else "Internal")).ap()

    x_hist = dri("x_hist", [NT_HIST * 128, H], F32)
    x_own = dri("x_own", [T_OWN, H], F32)
    tmask = dri("tmask", [128, NT], F32)
    wq = dri("wq", [D, H], F32)
    wk = dri("wk", [D, H], F32)
    wv = dri("wv", [D, H], F32)
    w_o = dri("w_o", [H, D], F32)
    Mm = dri("Mm", [D, D], F32)
    zv = dri("zv", [D, 1], F32)
    gate = dri("gate", [1, 1], F32)
    gw = dri("gw", [II, H], I32)
    uw = dri("uw", [II, H], I32)
    dw = dri("dw", [H, II], I32)
    gs_in = dri("gs_in", [128, NIT], F32)
    us_in = dri("us_in", [128, NIT], F32)
    ds_in = dri("ds_in", [128, NHT], F32)
    maskT_in = dri("maskT_in", [128, 128], F32)
    ident_in = dri("ident_in", [128, 128], F32)
    ones_row_in = dri("ones_row_in", [1, 128], F32)
    neg128_row_in = dri("neg128_row_in", [1, 128], F32)

    y_out = dro("y", [T_OWN, H], F32)
    xmid_sc = drt("xmid_sc", [T_OWN, H], F32)
    ffn_sc = drt("ffn_sc", [T_OWN, H], BF16)
    if debug:
        dbg_x2T = dro("dbg_x2T", [128, NHB * 1024], F32)
        dbg_hT = dro("dbg_hT", [128, 64 * 1024], F32)
        dbg_rsh = dro("dbg_rsh", [1, 1024], F32)
        dbg_qkv = dro("dbg_qkv", [128, 384], F32)
        dbg_pack = dro("dbg_pack", [128, 256], F32)
        dbg_outbf = dro("dbg_outbf", [128, 128], F32)
        dbg_sz = dro("dbg_sz", [128, 132], F32)
        dbg_tf = dro("dbg_tf", [128, 2048], F32)
        dbg_xmb = dro("dbg_xmb", [128, 2048], F32)

    sched = {e: [] for e in ENGS}
    ctr = {}

    def em(eng, fn):
        sched[eng].append(fn)

    es = ExitStack()
    C = es.enter_context
    with es:
        sx = C(nc.semaphore("sx"))
        swc = C(nc.semaphore("swc"))
        stp = C(nc.semaphore("stp"))
        sst = C(nc.semaphore("sst"))
        pe = C(nc.semaphore("pe"))
        sv = C(nc.semaphore("sv"))
        ss = C(nc.semaphore("ss"))
        sg = C(nc.semaphore("sg"))
        SEM = {"sx": sx, "swc": swc, "stp": stp, "sst": sst, "pe": pe, "sv": sv,
               "ss": ss, "sg": sg}

        def getsem(name):
            if name not in SEM:
                SEM[name] = C(nc.semaphore(name))
            return SEM[name]

        def W(eng, sem_name, val=None):
            if isinstance(sem_name, tuple):
                sem_name, val = sem_name
            if val is None or (isinstance(val, int) and val <= 0):
                return
            assert val <= ctr.get(sem_name, 0), (eng, sem_name, val, ctr.get(sem_name, 0))
            em(eng, lambda e, s=getsem(sem_name), v=val: e.wait_ge(s, v))

        def I(eng, fn, sem_name=None, by=1):
            if sem_name is None:
                em(eng, fn)
                return None
            ctr[sem_name] = ctr.get(sem_name, 0) + by
            v = ctr[sem_name]
            em(eng, lambda e, f=fn, s=getsem(sem_name), b=by: f(e).then_inc(s, b))
            return v

        def I2(eng, fn, slot):
            return (slot, I(eng, fn, slot, 16))

        ARENA_BYTES = 212800
        arena = C(nc.sbuf_tensor("arena", [128, ARENA_BYTES], U8))
        mem = Mem(nc, nc.lookup_mloc(arena).addr)

        o = [0]

        def P(nbytes, shape, dtype, align=64):
            off = (o[0] + align - 1) // align * align
            o[0] = off + nbytes
            return mem.at(off, shape, dtype)

        tones = P(128, [128, NT], F32)
        ones_row = P(512, [1, 128], F32)
        neg128_row = P(512, [1, 128], F32)
        ident = P(512, [128, 128], F32)
        gs_sb = P(256, [128, NIT], F32)
        us_sb = P(256, [128, NIT], F32)
        ds_sb = P(64, [128, NHT], F32)
        epsln = P(4, [128, 1], F32)
        Sz = P(528, [128, 132], F32)
        Sz_bf = P(264, [128, 132], BF16)
        statsP = [P(136, [128, 34], F32) for _ in range(6)]
        aggrP = [P(16, [128, 4], F32) for _ in range(6)]
        rstdP = [P(4, [128, 1], F32) for _ in range(6)]
        nmrP = [P(4, [128, 1], F32) for _ in range(6)]
        lnvP = [P(4, [128, 1], F32) for _ in range(6)]
        sg_in = P(4, [1, 1], F32)
        g01 = P(8, [1, 2], F32)
        gcol = P(4, [128, 1], F32)
        gmcol = P(4, [128, 1], F32)
        ones_i = P(2, [128, 1], BF16)
        maskT_bf = P(256, [128, 128], BF16)
        Mz_bf = P(264, [128, 132], BF16)
        rsh_row = P(4096, [1, 1024], F32)
        x2T = P(NHB * 1024 * 2, [128, NHB * 1024], BF16)
        PA = (o[0] + 127) // 128 * 128

        def A(off, shape, dtype):
            return mem.at(PA + off, shape, dtype)

        _aa = [0]

        def AA(nbytes, shape, dtype, align=64):
            off = (_aa[0] + align - 1) // align * align
            _aa[0] = off + nbytes
            return A(off, shape, dtype)

        wqkvT = AA(12288, [128, NHB * 384], BF16)
        NB = 6
        xb = [AA(8192, [128, H], F32) for _ in range(3)]
        lnb = [AA(4096, [128, H], BF16) for _ in range(3)]
        ln1T = [AA(4096, [128, H], BF16) for _ in range(3)]
        xr = [AA(8192, [128, H], F32) for _ in range(2)]
        qkv_sb = AA(1536, [128, 384], F32)
        kmin = AA(512, [128, 128], F32)
        qmin = AA(512, [128, 128], F32)
        ek = AA(512, [128, 128], F32)
        eq = AA(512, [128, 128], F32)
        pack = [AA(512, [128, 256], BF16) for _ in range(NB)]
        packT = [AA(512, [128, 256], BF16) for _ in range(NB)]
        v1 = [AA(288, [128, 132], BF16) for _ in range(NB)]
        amt = [AA(256, [128, 128], BF16) for _ in range(NB)]
        t_loc = AA(512, [128, 128], F32)
        comb_scr = AA(64, [128, 8], F32)
        out_bf = [AA(256, [128, 128], BF16) for _ in range(NB)]
        outT = [AA(256, [128, 128], BF16) for _ in range(NB)]
        attnout = AA(8192, [128, H], F32)
        _attnout_off = _aa[0] - 8192
        wnat = A(_attnout_off, [128, H], BF16)
        t_ln = AA(8192, [128, H], F32)
        xmid = [AA(8192, [128, H], F32) for _ in range(2)]
        x2b = AA(4096, [128, H], BF16)

        GB = max((_aa[0] + 63) // 64 * 64, 131072)
        hT = A(0, [128, NIT * 1024], BF16)
        gn = A(GB, [128, H], BF16)
        un = A(GB + 4096, [128, H], BF16)
        gT = [A(GB + 8192, [128, H], BF16), A(GB + 12288, [128, H], BF16)]
        uT = [A(GB + 16384, [128, H], BF16), A(GB + 20480, [128, H], BF16)]
        sgb = A(GB + 24576, [128, 1024], F32)
        w_oT = A(GB + 24576, [128, H], BF16)

        dn = [A(GB + 4096 * i, [128, 2048], BF16) for i in range(3)]
        dT = [A(GB + 12288 + 4096 * i, [128, 2048], BF16) for i in range(3)]
        tdb = A(GB + 24576, [128, 1024], F32)
        stage = A(GB + 28672, [128, 1024], BF16)
        assert PA + GB + 32768 <= ARENA_BYTES, (PA, GB)

        xmb = [A(8192 * i, [128, H], F32) for i in range(4)]
        ffb = [A(32768 + 8192 * i, [128, H], F32) for i in range(4)]
        tfP = [A(65536 + 8192 * i, [128, H], F32) for i in range(4)]



        p_big = C(nc.psum_tensor("p_big", [128, 2048], F32))
        p_aux = C(nc.psum_tensor("p_aux", [128, 2048], F32))
        pm = p_aux[:, 0:132]
        pa_ = p_aux[:, 512:640]
        pnd = p_aux[:, 1024:1156]
        ps_ = p_aux[:, 1536:1668]
        prs0 = p_aux[0:1, 1024:1536]
        prs1 = p_aux[0:1, 1536:2048]

        _ln_rot = [0]
        _ln_last_norm = [0] * 6

        def ln_chain(src_fn, dst_fn):
            par = _ln_rot[0] % 6
            _ln_rot[0] += 1
            stats, aggr = statsP[par], aggrP[par]
            rstd, nmr, lnv = rstdP[par], nmrP[par], lnvP[par]

            def f_stats(v, g=src_fn):
                s_ = g()
                for b in range(3):
                    v.bn_stats(stats[:, 6 * b:6 * b + 6], s_[:, 512 * b:512 * (b + 1)])
                v.bn_stats(stats[:, 18:24], s_[:, 1536:2048])
                v.drain()
                v.reduce_sum(aggr[:, 0:1], stats[:, 1:24:3], axis=AX.X)
                v.tensor_tensor(stats[:, 24:32], stats[:, 1:24:3], stats[:, 1:24:3], ALU.mult)
                v.reduce_sum(aggr[:, 2:3], stats[:, 2:24:3], axis=AX.X)
                v.drain()
                v.tensor_scalar_mul(aggr[:, 0:1], aggr[:, 0:1], -0.125)
                v.reduce_sum(aggr[:, 3:4], stats[:, 24:32], axis=AX.X)
                v.drain()
                v.tensor_scalar(aggr[:, 3:4], aggr[:, 3:4], 256.0 / 2048.0, None, ALU.mult)
                v.tensor_scalar(aggr[:, 2:3], aggr[:, 2:3], 1.0 / 2048.0, None, ALU.mult)
                v.tensor_tensor(stats[:, 32:33], aggr[:, 0:1], aggr[:, 0:1], ALU.mult)
                v.drain()
                v.tensor_add(aggr[:, 1:2], aggr[:, 2:3], aggr[:, 3:4])
                v.drain()
                return v.tensor_sub(aggr[:, 1:2], aggr[:, 1:2], stats[:, 32:33])

            W("vector", "ss", _ln_last_norm[par])
            a = I("vector", f_stats, "sv")
            W("scalar", "sv", a)

            def f_scalar(s, sf=src_fn, df=dst_fn):
                s.drain()
                s.activation(lnv[:, :], aggr[:, 1:2], AF.Ln, bias=epsln[:, 0:1], scale=1.0)
                s.drain()
                s.activation(rstd[:, :], lnv[:, :], AF.Exp, bias=0.0, scale=-0.5)
                s.drain()
                s.activation(nmr[:, :], aggr[:, 0:1], AF.Identity, bias=0.0,
                             scale=rstd[:, 0:1])
                s.drain()
                return s.activation(df(), sf(), AF.Identity,
                                    bias=nmr[:, 0:1], scale=rstd[:, 0:1])
            r = I("scalar", f_scalar, "ss")
            _ln_last_norm[par] = r
            return r

        I("sync", lambda e: e.dma_start(out=tones[:, :], in_=tmask[:, :]), "sx", 16)
        I("sync", lambda e: e.dma_start(out=ones_row[:, :], in_=ones_row_in[:, :]), "sx", 16)
        I("sync", lambda e: e.dma_start(out=neg128_row[:, :], in_=neg128_row_in[:, :]), "sx", 16)
        I("sync", lambda e: e.dma_start(out=ident[:, :], in_=ident_in[:, :]), "sx", 16)
        I("sync", lambda e: e.dma_start(out=gs_sb[:, :], in_=gs_in[:, :]), "sx", 16)
        I("sync", lambda e: e.dma_start(out=us_sb[:, :], in_=us_in[:, :]), "sx", 16)
        I("sync", lambda e: e.dma_start(out=ds_sb[:, :], in_=ds_in[:, :]), "sx", 16)
        sx_gate = I("sync", lambda e: e.dma_start(out=sg_in[:, :], in_=gate[:, :]), "sx", 16)
        W("vector", "sx", sx_gate)
        W("tensor", "sx", sx_gate)

        I("gpsimd", lambda e: e.dma_start(out=maskT_bf[:, :], in_=maskT_in[:, :]), "swcp", 16)
        I("gpsimd", lambda e: e.dma_start(out=Mz_bf[:, 0:128], in_=Mm[:, :]), "swcp", 16)
        swc_mz = I("gpsimd", lambda e: e.dma_start(out=Mz_bf[:, 128:129], in_=zv[:, :]), "swcp", 16)

        def vinit(v):
            v.memset(ones_i[:, :], 1.0)
            v.memset(epsln[:, :], LNEPS)
            v.memset(Sz[:, :], 0.0)
            v.memset(Sz_bf[:, :], 0.0)
            for vb in v1:
                v.memset(vb[:, 128:132], 0.0)
            return v.memset(Mz_bf[:, 129:132], 0.0)
        W("vector", "swcp", swc_mz)
        sv_init = I("vector", vinit, "sv")

        W("scalar", "sx", sx_gate)
        ss_eg = I("scalar", lambda e: e.activation(g01[0:1, 0:1], sg_in[0:1, 0:1], AF.Exp,
                                                   bias=0.0, scale=-1.0), "ss")
        W("vector", "ss", ss_eg)

        def vgate(v):
            v.tensor_scalar_add(g01[0:1, 0:1], g01[0:1, 0:1], 1.0)
            v.drain()
            return v.reciprocal(g01[0:1, 1:2], g01[0:1, 0:1])
        sv_g = I("vector", vgate, "sv")

        W("tensor", "sv", sv_g)
        pe_g = I("tensor", lambda e: e.matmul(p_aux[:, 0:1], ones_row[0:1, :],
                                              g01[0:1, 1:2], start=True, stop=True), "pe")
        W("vector", "pe", pe_g)

        def vgate2(v):
            v.tensor_copy(gcol[:, :], p_aux[:, 0:1])
            v.drain()
            return v.tensor_scalar(gmcol[:, :], gcol[:, :], -1.0, 1.0, ALU.mult, ALU.add)
        sv_g2 = I("vector", vgate2, "sv")

        def tp_qkvw(e, off):
            in3 = wnat[:, :].rearrange("p (b j) -> p b j", b=NHB)
            out3 = wqkvT[:, :].rearrange("q (b i) -> q b i", i=384)[:, :, off:off + 128]
            return e.dma_start(out=out3, in_=in3, transpose=True)

        cwq = I("gpsimd", lambda e: e.dma_start(out=wnat[:, :], in_=wq[:, :]), "swn", 16)
        W("sync", "swn", cwq)
        t1_ = I("sync", lambda e: tp_qkvw(e, 0), "stw", 16)
        W("gpsimd", "stw", t1_)
        cwk = I("gpsimd", lambda e: e.dma_start(out=wnat[:, :], in_=wk[:, :]), "swn", 16)
        W("sync", "swn", cwk)
        t2_ = I("sync", lambda e: tp_qkvw(e, 128), "stw", 16)
        W("gpsimd", "stw", t2_)
        cwv = I("gpsimd", lambda e: e.dma_start(out=wnat[:, :], in_=wv[:, :]), "swn", 16)
        W("sync", "swn", cwv)
        tp_wv = I("sync", lambda e: tp_qkvw(e, 256), "stw", 16)
        W("gpsimd", "stw", tp_wv)
        cwo = I("gpsimd", lambda e: e.dma_start(
            out=wnat[:, :].rearrange("p (b d) -> p b d", b=NHB),
            in_=w_o.rearrange("(b p) d -> p b d", p=128)), "swn", 16)
        W("sync", "swn", cwo)
        tp_wo = I("sync", lambda e: e.dma_start(
            out=w_oT[:, :].rearrange("q (b i) -> q b i", b=NHB),
            in_=wnat[:, :].rearrange("p (b j) -> p b j", b=NHB),
            transpose=True), "stw", 16)

        CG, CU, TG, TU = {}, {}, {}, {}
        PG, PU, SG, HT = {}, {}, {}, {}

        def emit_gate_cast(it):
            if it >= 1:
                W("gpsimd", TG[it - 1])
            CG[it] = I2("gpsimd", lambda e, i=it: e.dma_start(
                out=gn[:, :], in_=gw[i * 128:(i + 1) * 128, :]), "scg")
            if it >= 1:
                W("gpsimd", TU[it - 1])
            CU[it] = I2("gpsimd", lambda e, i=it: e.dma_start(
                out=un[:, :], in_=uw[i * 128:(i + 1) * 128, :]), "scu")

        def emit_gate_tp(it):
            b = it % 2
            W("sync", CG[it])
            if it >= 2:
                W("sync", "pe", PG[it - 2])
            TG[it] = I2("sync", lambda e, bb=b: e.dma_start(
                out=gT[bb][:, :].rearrange("q (b2 i) -> q b2 i", b2=NHB),
                in_=gn[:, :].rearrange("p (b2 j) -> p b2 j", b2=NHB),
                transpose=True), "stg%d" % b)
            W("sync", CU[it])
            if it >= 2:
                W("sync", "pe", PU[it - 2])
            TU[it] = I2("sync", lambda e, bb=b: e.dma_start(
                out=uT[bb][:, :].rearrange("q (b2 i) -> q b2 i", b2=NHB),
                in_=un[:, :].rearrange("p (b2 j) -> p b2 j", b2=NHB),
                transpose=True), "stu%d" % b)

        ln_done, qkvfree, st_done, su_done = {}, {}, {}, {}
        xmid_done, xmid_store, x2_tp, x2n_done = {}, {}, {}, {}
        evacA_ss, evacB_ss, post_norm = {}, {}, {}
        tpl, tpp, QK, ND, EX, PKV, AM, OB, TO = {}, {}, {}, {}, {}, {}, {}, {}, {}
        PM, PA_, ST_, WO1, WO2, XR = {}, {}, {}, {}, {}, {}

        pmnd = [p_big[:, 512:776], p_aux[:, 0:264]]

        def st0_load(t):
            own = t >= NT_HIST
            if t >= 3:
                W("sync", "ss", ln_done[t - 3])
            src = (x_own[(t - NT_HIST) * 128:(t - NT_HIST + 1) * 128, :] if own
                   else x_hist[t * 128:(t + 1) * 128, :])
            lx = I2("sync", lambda e, s=src, d=xb[t % 3]: e.dma_start(out=d[:, :], in_=s),
                    "sxb%d" % (t % 3))
            ln_done[("lx", t)] = lx

        def st1_ln(t):
            W("vector", ln_done[("lx", t)])
            if t == 0:
                W("vector", "sv", sv_g2)
            ln_done[t] = ln_chain(lambda b=t % 3: xb[b][:, :], lambda b=t % 3: lnb[b][:, :])

        def st2_tln(t):
            W("sync", "ss", ln_done[t])
            if t >= 3:
                W("sync", "pe", QK[t - 3])
            tpl[t] = I2("sync", lambda e, b=t % 3: e.dma_start(
                out=ln1T[b][:, :].rearrange("q (b2 i) -> q b2 i", b2=NHB),
                in_=lnb[b][:, :].rearrange("p (b2 j) -> p b2 j", b2=NHB),
                transpose=True), "stl%d" % (t % 3))

        def st3_qkv(t):
            own = t >= NT_HIST
            W("tensor", tpl[t])
            if t == 0:
                W("tensor", "stw", tp_wv)
                W("tensor", "stw", tp_wo)
            if t >= 1:
                W("tensor", "sv", qkvfree[t - 1])
            ncols = 384 if own else 256
            coff = 0 if own else 128

            def qkv_mm(e, b=t % 3, nc_=ncols, co=coff):
                mm = None
                for hb in range(NHB):
                    mm = e.matmul(p_big[:, 0:nc_], ln1T[b][:, hb * 128:(hb + 1) * 128],
                                  wqkvT[:, hb * 384 + co: hb * 384 + co + nc_],
                                  start=(hb == 0), stop=(hb == NHB - 1))
                return mm
            QK[t] = I("tensor", qkv_mm, "pe")

        def st4_evac(t):
            own = t >= NT_HIST
            ncols = 384 if own else 256
            W("vector", "pe", QK[t])
            if t >= 1:
                W("vector", "ss", EX[t - 1])

            def ev_sig(v, nc_=ncols, own_=own):
                v.drain()
                v.tensor_copy(qkv_sb[:, 0:nc_], p_big[:, 0:nc_])
                v.drain()
                last = v.tensor_scalar_min(kmin[:, :], qkv_sb[:, nc_ - 256:nc_ - 128], 0.0)
                if own_:
                    last = v.tensor_scalar_min(qmin[:, :], qkv_sb[:, 0:128], 0.0)
                return last
            qkvfree[t] = I("vector", ev_sig, "sv")
            if debug and t == NT_HIST:
                W("gpsimd", "sv", qkvfree[t])
                I("gpsimd", lambda e: e.dma_start(out=dbg_qkv[:, :], in_=qkv_sb[:, :]), "sdbg", 16)

            W("scalar", "sv", qkvfree[t])
            if t >= 1:
                W("scalar", "sv", PKV[t - 1])

            def s_exp(s, own_=own):
                s.drain()
                last = s.activation(ek[:, :], kmin[:, :], AF.Exp, bias=0.0, scale=1.0)
                if own_:
                    s.drain()
                    last = s.activation(eq[:, :], qmin[:, :], AF.Exp, bias=0.0, scale=1.0)
                return last
            EX[t] = I("scalar", s_exp, "ss")

        def st5_sig(t):
            own = t >= NT_HIST
            ncols = 384 if own else 256
            W("vector", "ss", EX[t])
            if t >= 4:
                W("vector", "pe", st_done[t - 4])
                if (t - 4) >= NT_HIST:
                    W("vector", tpp[t - 4])

            def v_sig(v, b=t % 6, nc_=ncols, own_=own, t_=t):
                v.drain()
                v.scalar_tensor_tensor(pack[b][:, 128:256], qkv_sb[:, nc_ - 256:nc_ - 128],
                                       1.0, ek[:, :], ALU.add, ALU.max)
                if own_:
                    v.scalar_tensor_tensor(pack[b][:, 0:128], qkv_sb[:, 0:128],
                                           1.0, eq[:, :], ALU.add, ALU.max)
                v.tensor_copy(v1[b][:, 0:128], qkv_sb[:, nc_ - 128:nc_])
                return v.tensor_copy(v1[b][:, 128:129], tones[:, t_:t_ + 1])
            PKV[t] = I("vector", v_sig, "sv")
            if debug and t == NT_HIST:
                W("gpsimd", "sv", PKV[t])
                I("gpsimd", lambda e: e.dma_start(out=dbg_pack[:, :], in_=pack[t % 6][:, :]), "sdbg", 16)

        def st6_tpack(t):
            W("sync", "sv", PKV[t])
            if t >= 4:
                W("sync", "pe", ND.get(t - 4))
            tpp[t] = I2("sync", lambda e, b=t % 6: e.dma_start(
                out=packT[b][:, :].rearrange("q (b2 i) -> q b2 i", b2=2),
                in_=pack[b][:, :].rearrange("p (b2 j) -> p b2 j", b2=2),
                transpose=True), "stpp%d" % (t % 6))

        def st7_ma(t):
            par = t % 2
            W("tensor", tpp[t])
            if t >= 2:
                W("tensor", "sv", OB.get(t - 2))
            PM[t] = I("tensor", lambda e, b=t % 6, p=par: e.matmul(
                pmnd[p][:, 0:132], packT[b][:, 0:128], Mz_bf[:, 0:132],
                start=True, stop=True), "pe")
            if t >= 1:
                W("tensor", "sv", AM.get(t - 1))
            PA_[t] = I("tensor", lambda e, b=t % 6: e.matmul(
                pa_, packT[b][:, 128:256], packT[b][:, 0:128],
                start=True, stop=True), "pe")
            W("vector", "pe", PA_[t])
            AM[t] = I("vector", lambda v, b=t % 6: v.tensor_mul(
                amt[b][:, :], pa_, maskT_bf[:, :]), "sv")

        def st8_ndst(t):
            own = t >= NT_HIST
            par = t % 2
            if own:
                W("tensor", "sv", AM[t])
                W("tensor", "sv", su_done[t - 1] if t >= 1 else sv_init)

                def nd_mm(e, b=t % 6, p=par):
                    e.matmul(pmnd[p][:, 132:264], amt[b][:, :], v1[b][:, 0:132],
                             start=True, stop=False)
                    return e.matmul(pmnd[p][:, 132:264], packT[b][:, 0:128],
                                    Sz_bf[:, 0:132], start=False, stop=True)
                ND[t] = I("tensor", nd_mm, "pe")
            elif t == 0:
                W("tensor", "sv", sv_init)

            W("tensor", "sv", PKV[t])
            if own and t >= 1:
                pass
            st_done[t] = I("tensor", lambda e, b=t % 6, own_=own, t_=t: e.matmul(
                ps_, pack[b][:, 128:256], v1[b][:, 0:132],
                start=(own_ or t_ == 0), stop=(own_ or t_ == NT_HIST - 1)), "pe")

            if own:
                W("vector", "pe", st_done[t])

                def v_su(v):
                    v.drain()
                    v.tensor_add(Sz[:, :], Sz[:, :], ps_)
                    v.drain()
                    return v.tensor_copy(Sz_bf[:, :], Sz[:, :])
                su_done[t] = I("vector", v_su, "sv")
            elif t == NT_HIST - 1:
                W("vector", "pe", st_done[t])

                def v_su0(v):
                    v.tensor_copy(Sz[:, :], ps_)
                    v.drain()
                    return v.tensor_copy(Sz_bf[:, :], Sz[:, :])
                su_done[t] = I("vector", v_su0, "sv")
                if debug:
                    W("gpsimd", "sv", su_done[t])
                    I("gpsimd", lambda e: e.dma_start(out=dbg_sz[:, :], in_=Sz[:, :]), "sdbg", 16)
            else:
                su_done[t] = None

        def st9_comb(t):
            par = t % 2
            W("vector", "pe", ND[t])

            def v_comb(v, b=t % 6, p=par):
                v.drain()
                v.tensor_scalar_add(comb_scr[:, 0:1], pmnd[p][:, 128:129], AEPS)
                v.tensor_scalar_add(comb_scr[:, 1:2], pmnd[p][:, 260:261], AEPS)
                v.drain()
                v.reciprocal(comb_scr[:, 4:5], comb_scr[:, 0:1])
                v.reciprocal(comb_scr[:, 5:6], comb_scr[:, 1:2])
                v.drain()
                v.tensor_mul(comb_scr[:, 2:3], comb_scr[:, 4:5], gcol[:, :])
                v.tensor_mul(comb_scr[:, 3:4], comb_scr[:, 5:6], gmcol[:, :])
                v.drain()
                v.tensor_scalar(t_loc[:, :], pmnd[p][:, 132:260], comb_scr[:, 3:4],
                                None, ALU.mult)
                v.drain()
                return v.scalar_tensor_tensor(out_bf[b][:, :], pmnd[p][:, 0:128],
                                              comb_scr[:, 2:3], t_loc[:, :],
                                              ALU.mult, ALU.add)
            OB[t] = I("vector", v_comb, "sv")
            if debug and t == NT_HIST:
                W("gpsimd", "sv", OB[t])
                I("gpsimd", lambda e: e.dma_start(out=dbg_outbf[:, :], in_=out_bf[0][:, :]), "sdbg", 16)

            W("sync", "sv", OB[t])
            TO[t] = I2("sync", lambda e, b=t % 6: e.dma_start(
                out=outT[b][:, :], in_=out_bf[b][:, :], transpose=True), "sto%d" % (t % 6))

        def st10_wo1(t):
            W("tensor", TO[t])
            if t >= 1:
                W("tensor", "ss", evacB_ss.get(t - 1))

            def wo1(e, b=t % 6):
                e.matmul(p_big[:, 1024:1536], outT[b][:, :], w_oT[:, 0:512],
                         start=True, stop=True)
                return e.matmul(p_big[:, 1536:2048], outT[b][:, :], w_oT[:, 512:1024],
                                start=True, stop=True)
            WO1[t] = I("tensor", wo1, "pe")
            W("scalar", "pe", WO1[t])
            evacA_ss[t] = I("scalar", lambda s: (s.drain(), s.activation(
                attnout[:, 0:1024], p_big[:, 1024:2048], AF.Identity,
                bias=0.0, scale=1.0))[1], "ss")
            if t >= NT_HIST + 2:
                W("sync", "sg", xmid_done[t - 2])
            XR[t] = I2("sync", lambda e, q=t - NT_HIST: e.dma_start(
                out=xr[q % 2][:, :], in_=x_own[q * 128:(q + 1) * 128, :]),
                "sxr%d" % ((t - NT_HIST) % 2))

        def st11_wo2(t):
            W("tensor", "ss", evacA_ss[t])

            def wo2(e, b=t % 6):
                e.matmul(p_big[:, 1024:1536], outT[b][:, :], w_oT[:, 1024:1536],
                         start=True, stop=True)
                return e.matmul(p_big[:, 1536:2048], outT[b][:, :], w_oT[:, 1536:2048],
                                start=True, stop=True)
            WO2[t] = I("tensor", wo2, "pe")
            W("scalar", "pe", WO2[t])
            evacB_ss[t] = I("scalar", lambda s: (s.drain(), s.activation(
                attnout[:, 1024:2048], p_big[:, 1024:2048], AF.Identity,
                bias=0.0, scale=1.0))[1], "ss")
            W("vector", "ss", evacB_ss[t])
            if t >= NT_HIST + 1:
                W("scalar", "sg", xmid_done[t - 1])
            post_norm[t] = ln_chain(lambda: attnout[:, :], lambda: t_ln[:, :])

        def st12_xm(t):
            tt = t - NT_HIST
            W("gpsimd", "ss", post_norm[t])
            W("gpsimd", XR[t])
            if t >= NT_HIST + 2:
                W("gpsimd", xmid_store[t - 2])
                W("gpsimd", "ss", x2n_done[t - 2])
            xm = I("gpsimd", lambda g, xp=tt % 2, q=tt: g.tensor_add(
                xmid[xp][:, :], xr[q % 2][:, :], t_ln[:, :]), "sg")
            xmid_done[t] = xm
            W("sync", "sg", xm)
            xmid_store[t] = I2("sync", lambda e, xp=tt % 2, q=tt: e.dma_start(
                out=xmid_sc[q * 128:(q + 1) * 128, :], in_=xmid[xp][:, :]), "sxs%d" % (tt % 2))

        def st13_ln2(t):
            tt = t - NT_HIST
            if t >= NT_HIST + 1:
                W("scalar", x2_tp[t - 1])
            W("vector", "sg", xmid_done[t])
            x2n_done[t] = ln_chain(lambda xp=tt % 2: xmid[xp][:, :], lambda: x2b[:, :])

        def st14_tx2(t):
            tt = t - NT_HIST
            W("sync", "ss", x2n_done[t])
            x2_tp[t] = I2("sync", lambda e, q=tt: e.dma_start(
                out=x2T[:, :].rearrange("q2 (b2 t2) -> q2 b2 t2", b2=NHB)[:, :, q * 128:(q + 1) * 128],
                in_=x2b[:, :].rearrange("p (b2 j) -> p b2 j", b2=NHB),
                transpose=True), "sx2%d" % (t % 2))

        STAGES = [st0_load, st1_ln, st2_tln, st3_qkv, st4_evac, st5_sig, st6_tpack,
                  st7_ma, st8_ndst, st9_comb, st10_wo1, st11_wo2, st12_xm,
                  st13_ln2, st14_tx2]
        OWN_ONLY = {6, 7, 9, 10, 11, 12, 13, 14}
        NSTG = len(STAGES)

        for s in range(NT + NSTG):
            for k in range(NSTG - 1, -1, -1):
                t = s - k
                if not (0 <= t < NT):
                    continue
                if k in OWN_ONLY and t < NT_HIST:
                    continue
                STAGES[k](t)
            if s == 10:
                emit_gate_cast(0)
                emit_gate_tp(0)
            if s == 20:
                emit_gate_cast(1)
                emit_gate_tp(1)

        for it in range(NIT):
            if it >= 2:
                emit_gate_cast(it)
                emit_gate_tp(it)
            b = it % 2

            W("tensor", TG[it])
            if it == 0:
                W("tensor", x2_tp[NT - 1])
            if it >= 1:
                W("tensor", "ss", SG[it - 1])

            def g_mm(e, bb=b):
                mm = None
                for hb in range(NHB):
                    for hf in range(2):
                        mm = e.matmul(p_big[:, hf * 512:(hf + 1) * 512],
                                      gT[bb][:, hb * 128:(hb + 1) * 128],
                                      x2T[:, hb * 1024 + hf * 512: hb * 1024 + (hf + 1) * 512],
                                      start=(hb == 0), stop=(hb == NHB - 1))
                return mm
            PG[it] = I("tensor", g_mm, "pe")

            W("tensor", TU[it])
            if it >= 1:
                W("tensor", "sv", HT[it - 1])

            def u_mm(e, bb=b):
                mm = None
                for hb in range(NHB):
                    for hf in range(2):
                        mm = e.matmul(p_big[:, 1024 + hf * 512:1024 + (hf + 1) * 512],
                                      uT[bb][:, hb * 128:(hb + 1) * 128],
                                      x2T[:, hb * 1024 + hf * 512: hb * 1024 + (hf + 1) * 512],
                                      start=(hb == 0), stop=(hb == NHB - 1))
                return mm
            PU[it] = I("tensor", u_mm, "pe")

            if it >= 1:
                def rs_mm(e, i=it - 1):
                    e.matmul(prs0, ones_i[:, 0:1], hT[:, i * 1024:i * 1024 + 512],
                             start=(i == 0), stop=False)
                    return e.matmul(prs1, ones_i[:, 0:1], hT[:, i * 1024 + 512:(i + 1) * 1024],
                                    start=(i == 0), stop=False)
                I("tensor", rs_mm, "pe")

            W("scalar", "pe", PG[it])
            if it >= 1:
                W("scalar", "sv", HT[it - 1])

            def s_silu(s, i=it):
                s.drain()
                return s.activation(sgb[:, :], p_big[:, 0:1024], AF.Silu,
                                    bias=0.0, scale=gs_sb[:, i:i + 1])
            SG[it] = I("scalar", s_silu, "ss")

            W("vector", "pe", PU[it])
            W("vector", "ss", SG[it])
            if it == 0:
                W("vector", xmid_store[NT - 1])
                W("vector", xmid_store[NT - 2])

            def v_h(v, i=it):
                v.drain()
                return v.scalar_tensor_tensor(hT[:, i * 1024:(i + 1) * 1024],
                                              p_big[:, 1024:2048], us_sb[:, i:i + 1],
                                              sgb[:, :], ALU.mult, ALU.mult)
            HT[it] = I("vector", v_h, "sv")

        if debug:
            W("gpsimd", x2_tp[NT - 1])
            I("gpsimd", lambda e: e.dma_start(out=dbg_x2T[:, :], in_=x2T[:, :]), "sdbg", 16)
            W("gpsimd", "sv", HT[63])
            I("gpsimd", lambda e: e.dma_start(out=dbg_hT[:, :], in_=hT[:, :]), "sdbg", 16)

        def rs_fin(e):
            e.matmul(prs0, ones_i[:, 0:1], hT[:, 63 * 1024:63 * 1024 + 512],
                     start=False, stop=True)
            return e.matmul(prs1, ones_i[:, 0:1], hT[:, 63 * 1024 + 512:64 * 1024],
                            start=False, stop=True)
        W("tensor", "sv", HT[63])
        rsf = I("tensor", rs_fin, "pe")
        W("vector", "pe", rsf)
        rsv = I("vector", lambda v: v.tensor_copy(rsh_row[0:1, :], p_aux[0:1, 1024:2048]), "sv")

        if debug:
            W("gpsimd", "sv", rsv)
            I("gpsimd", lambda e: e.dma_start(out=dbg_rsh[:, :], in_=rsh_row[0:1, :]), "sdbg", 16)

        CD, TD, PD, TDv, DQ = {}, {}, {}, {}, {}
        SF, TRC = {}, {}

        def emit_down_cast(k):
            ht, q = divmod(k, 4)
            bq = k % 3
            if k <= 2:
                W("gpsimd", TU[NIT - 1])
                W("gpsimd", "pe", PU[NIT - 1])
            if k >= 3:
                W("gpsimd", TD[k - 3])
            CD[k] = I2("gpsimd", lambda e, h=ht, qq=q: e.dma_start(
                out=dn[bq][:, :], in_=dw[h * 128:(h + 1) * 128, qq * 2048:(qq + 1) * 2048]),
                "scd%d" % bq)

        def emit_down_tp(k):
            bq = k % 3
            W("sync", CD[k])
            if k <= 2:
                W("sync", "pe", PU[NIT - 1])
            if k >= 3:
                W("sync", "pe", DQ[k - 3])
            TD[k] = I2("sync", lambda e: e.dma_start(
                out=dT[bq][:, :].rearrange("q2 (b2 i) -> q2 b2 i", b2=NHB),
                in_=dn[bq][:, :].rearrange("p (b2 j) -> p b2 j", b2=NHB),
                transpose=True), "std%d" % bq)

        for k in range(3):
            emit_down_cast(k)
            emit_down_tp(k)

        pdb = [p_big[:, 0:1024], p_aux[:, 1024:2048]]
        for ht in range(NHT):
            dpar = ht % 2
            if ht >= 2:
                W("tensor", "sv", TDv[ht - 2])
            elif ht == 0:
                W("tensor", "ss", SG[NIT - 1])
                W("tensor", "sv", rsv)
            else:
                W("tensor", "sv", rsv)

            for q in range(4):
                k = ht * 4 + q
                W("tensor", TD[k])

                def d_mm(e, qq=q, bq=k % 3, dp=dpar):
                    mm = None
                    for j in range(NHB):
                        ib = qq * 16 + j
                        for hf in range(2):
                            mm = e.matmul(pdb[dp][:, hf * 512:(hf + 1) * 512],
                                          dT[bq][:, j * 128:(j + 1) * 128],
                                          hT[:, ib * 1024 + hf * 512: ib * 1024 + (hf + 1) * 512],
                                          start=(ib == 0), stop=False)
                    return mm
                DQ[k] = I("tensor", d_mm, "pe")
                if k + 3 < NHT * 4:
                    emit_down_cast(k + 3)
                    emit_down_tp(k + 3)

            def d_corr(e, dp=dpar):
                e.matmul(pdb[dp][:, 0:512], neg128_row[0:1, :], rsh_row[0:1, 0:512],
                         start=False, stop=True)
                return e.matmul(pdb[dp][:, 512:1024], neg128_row[0:1, :],
                                rsh_row[0:1, 512:1024], start=False, stop=True)
            PD[ht] = I("tensor", d_corr, "pe")

            W("vector", "pe", PD[ht])
            if ht >= 1:
                W("vector", "pe", TRC[ht - 1])

            def v_td(v, h=ht, dp=dpar):
                v.drain()
                return v.tensor_scalar(tdb[:, :], pdb[dp][:, 0:1024], ds_sb[:, h:h + 1],
                                       None, ALU.mult)
            TDv[ht] = I("vector", v_td, "sv")

            W("tensor", "sv", TDv[ht])
            if ht >= 1:
                W("tensor", SF[ht - 1])
            cps = []
            for j in range(8):
                scr = p_big[:, 1024 + (j % 2) * 512: 1024 + (j % 2) * 512 + 128]
                if j >= 2:
                    W("tensor", "sv", cps[j - 2])
                trc = I("tensor", lambda e, s=scr, j_=j, h=ht: e.transpose(
                    s, tdb[:, j_ * 128:(j_ + 1) * 128], ident[:, :]), "pe")
                if j == 7:
                    TRC[ht] = trc
                W("vector", "pe", trc)
                cp = I("vector", lambda v, s=scr, j_=j: (v.drain(), v.tensor_copy(
                    stage[:, j_ * 128:(j_ + 1) * 128], s))[1], "sv")
                cps.append(cp)

            W("sync", "sv", cps[-1])
            SF[ht] = I2("sync", lambda e, h=ht: e.dma_start(
                out=ffn_sc.rearrange("(t2 p) (h2 j) -> p t2 h2 j", p=128, j=128)[:, :, h, :],
                in_=stage[:, :].rearrange("p (t2 j) -> p t2 j", j=128)), "ssf")

        YS, Yv, LX1, LX2 = {}, {}, {}, {}

        def p4_load(tt):
            bb = tt % 4
            if tt >= 4:
                W("sync", YS[tt - 4])
                W("sync", "sg", Yv[tt - 4])
                W("gpsimd", YS[tt - 4])
                W("gpsimd", "sg", Yv[tt - 4])
            LX1[tt] = I2("sync", lambda e, q=tt, b2=bb: e.dma_start(
                out=xmb[b2][:, :], in_=xmid_sc[q * 128:(q + 1) * 128, :]), "sxl%d" % bb)
            LX2[tt] = I2("gpsimd", lambda e, q=tt, b2=bb: e.dma_start(
                out=ffb[b2][:, :], in_=ffn_sc[q * 128:(q + 1) * 128, :]), "sfb%d" % bb)

        W("sync", "pe", TRC[NHT - 1])
        W("sync", SF[NHT - 1])
        W("gpsimd", "pe", TRC[NHT - 1])
        W("gpsimd", SF[NHT - 1])
        for tt in range(4):
            p4_load(tt)
        for tt in range(NT_OWN):
            b = tt % 4
            W("vector", LX2[tt])
            if tt >= 4:
                W("scalar", "sg", Yv[tt - 4])
            ffn_ln = ln_chain(lambda bb=b: ffb[bb][:, :], lambda bb=b: tfP[bb][:, :])
            W("gpsimd", "ss", ffn_ln)
            W("gpsimd", LX1[tt])

            def v_y(g, bb=b):
                return g.tensor_add(ffb[bb][:, :], xmb[bb][:, :], tfP[bb][:, :])
            Yv[tt] = I("gpsimd", v_y, "sg")
            if debug and tt == 0:
                W("gpsimd", "ss", ffn_ln)
                I("gpsimd", lambda e: e.dma_start(out=dbg_tf[:, :], in_=tfP[0][:, :]), "sdbg", 16)
                I("gpsimd", lambda e: e.dma_start(out=dbg_xmb[:, :], in_=xmb[0][:, :]), "sdbg", 16)

            W("sync", "sg", Yv[tt])
            YS[tt] = I2("sync", lambda e, q=tt, bb=b: e.dma_start(
                out=y_out[q * 128:(q + 1) * 128, :], in_=ffb[bb][:, :]), "sys%d" % b)
            if tt + 4 < NT_OWN:
                p4_load(tt + 4)

        for tt in range(NT_OWN - 4, NT_OWN):
            W("sync", YS[tt])

        block = C(nc.Block())

        def run(lst):
            def f(e):
                for fn in lst:
                    fn(e)
            return f
        block.sync(run(sched["sync"]))
        block.gpsimd(run(sched["gpsimd"]))
        block.scalar(run(sched["scalar"]))
        block.vector(run(sched["vector"]))
        block.tensor(run(sched["tensor"]))

    return nc


_NC_CACHE = None


def _get_nc():
    global _NC_CACHE
    if _NC_CACHE is None:
        _NC_CACHE = build_program()
    return _NC_CACHE


def kernel(**inputs):
    X = np.asarray(inputs["L0_attn_hidden_states"], dtype=np.float32)
    wq = np.ascontiguousarray(inputs["L0_attn_w_q"], dtype=np.float32)
    wk = np.ascontiguousarray(inputs["L0_attn_w_k"], dtype=np.float32)
    wv = np.ascontiguousarray(inputs["L0_attn_w_v"], dtype=np.float32)
    w_o = np.ascontiguousarray(inputs["L0_attn_w_o"], dtype=np.float32)
    gate = np.asarray(inputs["L0_attn_gate"], dtype=np.float32).reshape(1, 1)
    Mm = np.ascontiguousarray(inputs["L0_attn_M"], dtype=np.float32)
    zv = np.asarray(inputs["L0_attn_z"], dtype=np.float32).reshape(128, 1)
    gw = np.ascontiguousarray(inputs["L0_ffn_gate_w"], dtype=np.int32)
    uw = np.ascontiguousarray(inputs["L0_ffn_up_w"], dtype=np.int32)
    dw = np.ascontiguousarray(inputs["L0_ffn_down_w"], dtype=np.int32)
    gs = np.ascontiguousarray(np.asarray(inputs["L0_ffn_gate_scale"], dtype=np.float32)
                              .reshape(NIT, 128).T)
    us = np.ascontiguousarray(np.asarray(inputs["L0_ffn_up_scale"], dtype=np.float32)
                              .reshape(NIT, 128).T)
    ds = np.ascontiguousarray(np.asarray(inputs["L0_ffn_down_scale"], dtype=np.float32)
                              .reshape(NHT, 128).T)

    maskT = np.triu(np.ones((128, 128), dtype=np.float32))
    ident = np.eye(128, dtype=np.float32)
    ones_row = np.ones((1, 128), dtype=np.float32)
    neg128_row = np.full((1, 128), -128.0, dtype=np.float32)

    in_maps = []
    for c in range(8):
        bb, j = divmod(c, 4)
        xh = np.zeros((NT_HIST * 128, H), dtype=np.float32)
        nh = j * 1024
        if nh:
            xh[:nh] = X[bb, :nh]
        tm = np.zeros((128, NT), dtype=np.float32)
        tm[:, : nh // 128] = 1.0
        tm[:, NT_HIST:] = 1.0
        xo = np.ascontiguousarray(X[bb, j * 1024:(j + 1) * 1024])
        in_maps.append({
            "x_hist": xh, "x_own": xo, "tmask": tm,
            "wq": wq, "wk": wk, "wv": wv, "w_o": w_o, "Mm": Mm, "zv": zv,
            "gate": gate, "gw": gw, "uw": uw, "dw": dw,
            "gs_in": gs, "us_in": us, "ds_in": ds,
            "maskT_in": maskT, "ident_in": ident,
            "ones_row_in": ones_row, "neg128_row_in": neg128_row,
        })

    nc = _get_nc()
    res = run_bass_kernel_spmd(nc, in_maps, core_ids=list(range(8)))
    out = np.empty((2, 4096, H), dtype=np.float32)
    for c in range(8):
        bb, j = divmod(c, 4)
        out[bb, j * 1024:(j + 1) * 1024] = res.results[c]["y"]
    return out


if __name__ == "__main__":
    build_program()
    print("build ok")
